# version 03 (final)
# baseline (speedup 1.0000x reference)
import numpy as np
from contextlib import ExitStack
from collections import defaultdict

import concourse.bass as bass
import concourse.mybir as mybir
from concourse.bass_utils import run_bass_kernel_spmd

F32 = mybir.dt.float32
BF16 = mybir.dt.bfloat16
AF = mybir.ActivationFunctionType
ALU = mybir.AluOpType
AX = mybir.AxisListType.X

NCORE = 8
NSEQ = 4
T = 2048
NT = 16
D = 1024
KC = 8
NE = 12
HID = 4096
MC = 32
NH = 16
HD = 64
NP = 8
MACT = 8
NMACRO = 2
EPS_LN = 1e-5
INVTAU = 1.0 / (1.0 + 1e-8)
JW = 512
NJW = HID // JW
SBYTES = 212864

STAGES = ("ffn1", "attn", "ffn2")
TAP = False
DIAG = False
ATTN_SUB = 5
FFN2_SUB = "abc"


class Walk:

    def __init__(self, engname, eng):
        self.engname = engname
        self.eng = eng
        self.c = defaultdict(int)
        self.waited = defaultdict(int)

    spacers = {}

    def op(self, target, f=None, wait=(), inc=(), spaced=False):
        if self.engname == target:
            e = self.eng
            for s, v in wait:
                if v > self.waited[s]:
                    e.wait_ge(s, v)
                    self.waited[s] = v
            ins = f(e) if f is not None else None
            if inc and spaced and target in Walk.spacers:
                ins = Walk.spacers[target](e)
            for s, n in inc:
                if ins is None:
                    ins = e.engine_nop()
                ins = ins.then_inc(s, n)
        for s, n in inc:
            self.c[s] += n

    def val(self, s):
        return self.c[s]


def host_prep(inputs):
    d = {}
    d["__x"] = np.ascontiguousarray(inputs["L0_attn_x"].reshape(32 * T, D))
    d["w1t"] = np.ascontiguousarray(inputs["L0_ffn_W1"].transpose(0, 2, 1))
    d["w2t"] = np.ascontiguousarray(inputs["L0_ffn_W2"].transpose(0, 2, 1))
    d["qkvwt"] = np.ascontiguousarray(inputs["L0_attn_qkv_w"].T)
    d["p1wt"] = np.ascontiguousarray(inputs["L0_attn_p1_w"].T)
    d["p2wt"] = np.ascontiguousarray(inputs["L0_attn_p2_w"].T)
    d["pwt"] = np.ascontiguousarray(inputs["L0_ffn_P_w"].T)
    d["ucat"] = np.ascontiguousarray(np.concatenate(
        [inputs["L0_ffn_U1"].T, inputs["L0_ffn_U2"].T, inputs["L0_ffn_U3"].T], axis=1))
    d["b2"] = np.ascontiguousarray(inputs["L0_ffn_b2"])
    d["qkvbr"] = np.ascontiguousarray(inputs["L0_attn_qkv_b"].reshape(24, 128))
    d["p1br"] = np.ascontiguousarray(inputs["L0_attn_p1_b"].reshape(8, 128))
    d["p2brow"] = np.ascontiguousarray(inputs["L0_attn_p2_b"].reshape(1, D))
    d["kng"] = np.ascontiguousarray(inputs["L0_attn_kn_g"].reshape(1, HD))
    d["knb"] = np.ascontiguousarray(inputs["L0_attn_kn_b"].reshape(1, HD))
    d["vng"] = np.ascontiguousarray(inputs["L0_attn_vn_g"].reshape(1, HD))
    d["vnb"] = np.ascontiguousarray(inputs["L0_attn_vn_b"].reshape(1, HD))
    d["temp"] = np.ascontiguousarray(inputs["L0_attn_temperature"].reshape(1, NH))
    d["attnb"] = np.ascontiguousarray(inputs["L0_attn_attn_bias"].reshape(NP, 128, HD))
    return d


def build_program():
    nc = bass.Bass("TRN2", target_bir_lowering=False, detect_race_conditions=False)

    x_in = nc.dram_tensor("x", [NSEQ * T, D], F32, kind="ExternalInput")
    w1t = nc.dram_tensor("w1t", [NE, D, HID], F32, kind="ExternalInput")
    w2t = nc.dram_tensor("w2t", [NE, HID, D], F32, kind="ExternalInput")
    qkvwt = nc.dram_tensor("qkvwt", [D, 3 * D], F32, kind="ExternalInput")
    p1wt = nc.dram_tensor("p1wt", [D, D], F32, kind="ExternalInput")
    p2wt = nc.dram_tensor("p2wt", [D, D], F32, kind="ExternalInput")
    pwt = nc.dram_tensor("pwt", [D, HD], F32, kind="ExternalInput")
    ucat_d = nc.dram_tensor("ucat", [HD, 36], F32, kind="ExternalInput")
    b2_d = nc.dram_tensor("b2", [NE, D], F32, kind="ExternalInput")
    qkvbr_d = nc.dram_tensor("qkvbr", [24, 128], F32, kind="ExternalInput")
    p1br_d = nc.dram_tensor("p1br", [8, 128], F32, kind="ExternalInput")
    p2brow_d = nc.dram_tensor("p2brow", [1, D], F32, kind="ExternalInput")
    kng_d = nc.dram_tensor("kng", [1, HD], F32, kind="ExternalInput")
    knb_d = nc.dram_tensor("knb", [1, HD], F32, kind="ExternalInput")
    vng_d = nc.dram_tensor("vng", [1, HD], F32, kind="ExternalInput")
    vnb_d = nc.dram_tensor("vnb", [1, HD], F32, kind="ExternalInput")
    temp_d = nc.dram_tensor("temp", [1, NH], F32, kind="ExternalInput")
    attnb_d = nc.dram_tensor("attnb", [NP, 128, HD], F32, kind="ExternalInput")
    out_d = nc.dram_tensor("out", [NSEQ * T, D], F32, kind="ExternalOutput")

    diag_d = {}
    if DIAG:
        for i in range(5):
            diag_d[i] = nc.dram_tensor(f"diag_x{i}", [128, NT * D], F32,
                                       kind="ExternalOutput")
        diag_d["dw"] = nc.dram_tensor("diag_dw", [128, MACT * 36], F32,
                                      kind="ExternalOutput")
        diag_d["hj"] = nc.dram_tensor("diag_hj", [128, MACT * JW], F32,
                                      kind="ExternalOutput")
        diag_d["xh"] = nc.dram_tensor("diag_xh", [128, KC * T], BF16,
                                      kind="ExternalOutput")
    tap_d = {}
    if TAP:
        for nm, sh, dt in (("xhat", [128, KC * T], BF16), ("q", [128, KC * T], BF16),
                           ("kraw0", [128, T], BF16), ("ktm0", [128, T], BF16),
                           ("vtm0", [128, T], BF16), ("kvstat0", [128, 96], F32),
                           ("kv", [128, NP * HD], BF16), ("u", [128, KC * T], BF16),
                           ("hp", [128, KC * T], BF16),
                           ("xin", [128, NT * D], F32),
                           ("fdw", [128, MACT * 36], F32), ("fd3", [12, MACT * 128], BF16),
                           ("fxh", [128, KC * T], BF16), ("fg", [128, MC * 1024], BF16),
                           ("fh", [128, MACT * JW], F32), ("fx0", [128, NT * D], F32)):
            tap_d[nm] = nc.dram_tensor("tap_" + nm, sh, dt, kind="ExternalOutput")
    w1b = nc.dram_tensor("w1b", [NE, D, HID], BF16, kind="Internal")
    w2b = nc.dram_tensor("w2b", [NE, HID, D], BF16, kind="Internal")
    qkvb_s = nc.dram_tensor("qkvb_s", [D, 3 * D], BF16, kind="Internal")
    p1b_s = nc.dram_tensor("p1b_s", [D, D], BF16, kind="Internal")
    p2b_s = nc.dram_tensor("p2b_s", [D, D], BF16, kind="Internal")

    SB0 = 16512
    def A(name, shape, dt, offset):
        return nc.alloc_sbuf_tensor_at(name, shape, dt, offset=offset + SB0)
    x_tm = A("x_tm", [128, NT * D], F32, offset=0)
    xhat = A("xhat", [128, KC * T], BF16, offset=65536)
    OFF_C = 98304
    co = [OFF_C]
    def calloc(name, shape, dt, min_bytes=None):
        nb = {F32: 4, BF16: 2}[dt]
        sz = shape[-1] * nb if min_bytes is None else min_bytes
        t = A(name, shape, dt, offset=co[0])
        co[0] = (co[0] + sz + 31) & ~31
        return t
    consts = calloc("consts", [128, 8], F32)
    qkvb_c = calloc("qkvb_c", [128, 24], F32)
    p1b_c = calloc("p1b_c", [128, 8], F32)
    p2brow_b = calloc("p2brow_b", [1, D], BF16)
    onesrow = calloc("onesrow", [1, 128], BF16)
    onesf = calloc("onesf", [1, 128], F32)
    onesq = calloc("onesq", [1, 128], F32)
    iden_b = calloc("iden_b", [128, 128], BF16)
    iden_f = calloc("iden_f", [128, 128], F32)
    pwb = calloc("pwb", [128, KC * HD], BF16)
    ucat_sb = calloc("ucat_sb", [64, 36], F32, min_bytes=160)
    b2_sb = calloc("b2_sb", [12, D], BF16)
    gbc = calloc("gbc", [128, 2 * 128], BF16)
    gbc2 = calloc("gbc2", [128, 2 * 128], BF16)
    tempq = calloc("tempq", [128, NP], F32)
    attnb_sb = calloc("attnb_sb", [128, NP * HD], BF16)
    dw_sb = calloc("dw_sb", [128, MACT * 36], F32)
    dw3t = calloc("dw3t", [12, MACT * 128], BF16)
    astats = calloc("astats", [128, 2 * NT], F32)
    fstats = calloc("fstats", [128, 2], F32)
    lncol = calloc("lncol", [128, 6], F32)
    rt_sc = calloc("rt_sc", [128, 176], F32)
    xa_sc = calloc("xa_sc", [128, 64], F32)
    xat_sc = calloc("xat_sc", [64, 128], F32)
    kvsb = calloc("kvsb", [128, NP * HD], BF16)
    kvstat = calloc("kvstat", [128, 96], F32)
    vjunk = calloc("vjunk", [128, 1], F32)
    sjunk = calloc("sjunk", [1, 2], F32)
    Walk.spacers = {
        "vector": lambda e: e.memset(vjunk[:, :], 0.0),
        "scalar": lambda e: e.memzero(sjunk[0:1, 0:1]),
    }
    OFF_D = (co[0] + 63) & ~63
    assert OFF_D <= 116736, f"C region too big: ends {co[0]}"
    DSZ = SBYTES - OFF_D

    g_fm = A("g_fm", [128, MC * 1024], BF16, offset=OFF_D)
    wbuf1 = [A(f"wbuf1_{i}", [128, KC * JW], BF16, offset=OFF_D + 65536 + i * 8192)
             for i in range(2)]
    h_jw = A("h_jw", [128, MACT * JW], F32, offset=OFF_D + 81920)
    sqtmp = A("sqtmp", [128, D], F32, offset=OFF_D + 81920)
    xh_t = A("xh_t", [128, D], BF16, offset=OFF_D + 94208)
    wbuf2 = [A(f"wbuf2_{i}", [128, 16 * 512], BF16, offset=OFF_D + 65536 + i * 16384)
             for i in range(2)]
    q_fm = A("q_fm", [128, KC * T], BF16, offset=OFF_D)
    u_fm = A("u_fm", [128, KC * T], BF16, offset=OFF_D + 32768)
    hp_fm = A("hp_fm", [128, KC * T], BF16, offset=OFF_D)
    kpair = A("kpair", [128, T], BF16, offset=OFF_D + 65536)
    vpair = A("vpair", [128, T], BF16, offset=OFF_D + 69632)
    k_tm = A("k_tm", [128, T], BF16, offset=OFF_D + 73728)
    v_tm = A("v_tm", [128, T], BF16, offset=OFF_D + 77824)
    sqk = A("sqk", [128, T], BF16, offset=OFF_D + 81920)
    xhrec = A("xhrec", [128, D], F32, offset=OFF_D + 81920)
    dgap = A("dgap", [128, 512], BF16, offset=OFF_D + 84992)
    wbufa = [A(f"wbufa_{i}", [128, KC * 256], BF16, offset=OFF_D + 86016 + i * 4096)
             for i in range(2)]
    qkvb_r = A("qkvb_r", [24, 128], F32, offset=OFF_D + 65536)
    p1b_r = A("p1b_r", [8, 128], F32, offset=OFF_D + 66048)
    grow = A("grow", [1, 4 * HD], F32, offset=OFF_D + 66560)
    temp_r = A("temp_r", [1, NH], F32, offset=OFF_D + 67584)
    iota_f = A("iota_f", [128, 128], F32, offset=OFF_D + 67648)
    assert OFF_D + 98304 <= SBYTES, OFF_D

    pb = [nc.alloc_psum_tensor(f"pb{i}", [128, 512], F32) for i in range(6)]
    pt = [nc.alloc_psum_tensor(f"pt{i}", [128, 1024], BF16) for i in range(2)]

    es = ExitStack()
    sems = {n: es.enter_context(nc.semaphore(n)) for n in
            ("conv", "x", "w1a", "w1b", "w2a", "w2b", "wqa", "wqb", "mm", "vec", "act", "pool")}
    S_CONV, S_X, S_MM, S_V, S_A, S_P = (sems["conv"], sems["x"], sems["mm"],
                                        sems["vec"], sems["act"], sems["pool"])
    S_W1 = [sems["w1a"], sems["w1b"]]
    S_W2 = [sems["w2a"], sems["w2b"]]
    S_WQ = [sems["wqa"], sems["wqb"]]

    xhat_v = xhat.ap().rearrange("p (kc t) -> p kc t", kc=KC)
    g_v = g_fm.ap().rearrange("p (mc t) -> p mc t", mc=MC)

    def walk(w: Walk):
        st = {
            "w1_slice": 0, "w1_end": {}, "w2_slice": 0, "w2_end": {},
            "wq_slice": 0, "wq_end": {}, "grp": 0, "mac": {},
            "gelu": [0] * MACT, "ptev": [0, 0], "rt_done": 0,
            "x_ready": [], "store_done": 0, "seq": 0, "tapv": 0,
            "diag_n": 0, "diag_gate": 0,
        }

        def diag(xwait):
            if not DIAG or st["seq"] != 0 or st["diag_n"] >= 5:
                return
            i = st["diag_n"]
            st["diag_n"] += 1
            w.op("sync", lambda e, d=diag_d[i]: e.dma_start(d[:, :], x_tm.ap()[:, :]),
                 wait=list(xwait) + [(S_X, w.val(S_X))], inc=[(S_X, 16)])
            st["diag_gate"] = w.val(S_X)

        def tap(name, src, wt):
            if not TAP or st["seq"] != 0:
                return
            w.op("scalar", lambda e, d=tap_d[name], s=src: e.dma_start(d[:, :], s),
                 wait=list(wt) + [(S_P, w.val(S_P))], inc=[(S_P, 16)])
            st["tapv"] = w.val(S_P)

        w.op("gpsimd", lambda e: e.iota(iota_f[:, :], [[1, 128]], channel_multiplier=-1,
                                        allow_small_or_imprecise_dtypes=True), inc=[(S_P, 1)])
        w.op("gpsimd", lambda e: e.dma_start(
            pwb.ap().rearrange("p (kc c) -> p kc c", kc=KC),
            pwt[:, :].rearrange("(kc p) c -> p kc c", p=128)), inc=[(S_CONV, 16)])
        w.op("gpsimd", lambda e: e.dma_start(p2brow_b[:, :], p2brow_d[:, :]),
             wait=[(S_CONV, w.val(S_CONV))], inc=[(S_CONV, 16)])
        w.op("gpsimd", lambda e: e.dma_start(b2_sb[:, :], b2_d[:, :]),
             wait=[(S_CONV, w.val(S_CONV))], inc=[(S_CONV, 16)])
        for dst, src in ((ucat_sb, ucat_d), (temp_r, temp_d)):
            w.op("sync", lambda e, d=dst, s=src: e.dma_start(d[:, :], s[:, :]),
                 wait=[(S_X, w.val(S_X))], inc=[(S_X, 16)])
        for i, src in enumerate((kng_d, knb_d, vng_d, vnb_d)):
            w.op("sync", lambda e, i=i, s=src: e.dma_start(grow[:, i * HD:(i + 1) * HD], s[:, :]),
                 wait=[(S_X, w.val(S_X))], inc=[(S_X, 16)])
        w.op("gpsimd", lambda e: e.dma_start(
            attnb_sb.ap().rearrange("p (n c) -> p n c", n=NP),
            attnb_d[:, :, :].rearrange("n p c -> p n c")),
             wait=[(S_CONV, w.val(S_CONV))], inc=[(S_CONV, 16)])
        v_pw = w.val(S_CONV)
        v_small = w.val(S_X)

        def _memsets(e):
            e.memset(consts[:, 0:1], EPS_LN)
            e.memset(consts[:, 1:2], -0.5)
            e.memset(onesrow[:, :], 1.0)
            e.memset(onesf[:, :], 1.0)
            return e.memset(onesq[:, :], 1.0 / T)
        def _init_scratch(e):
            for tns in (lncol, fstats, rt_sc, xa_sc, dw_sb, astats, kvstat,
                        qkvb_c, p1b_c, xh_t, xhat, g_fm,
                        dgap, dw3t, kvsb, vjunk):
                ins = e.memset(tns[:, :], 0.0)
            e.memset(xat_sc[:, :], 0.0)
            return ins

        def _init_scratch_late(e):
            e.memset(h_jw[:, :], 0.0)
            e.memset(wbuf1[0][:, :], 0.0)
            return e.memset(wbuf1[1][:, :], 0.0)
        w.op("vector", _memsets)
        w.op("vector", lambda e: e.tensor_scalar(iden_b[:, :], iota_f[:, :], 0.0, None,
                                                 op0=ALU.is_equal), wait=[(S_P, 1)])
        w.op("vector", lambda e: e.tensor_scalar(iden_f[:, :], iota_f[:, :], 0.0, None,
                                                 op0=ALU.is_equal))
        w.op("vector", _init_scratch, inc=[(S_V, 1)], spaced=True)
        v_iden = w.val(S_V)

        v_bcols = None
        for rows, src_d, dst_c in ((24, qkvbr_d, qkvb_c), (8, p1br_d, p1b_c)):
            w.op("sync", lambda e, r=rows, s=src_d: e.dma_start(iota_f[0:r, :], s[:, :]),
                 wait=([(S_V, v_iden)] if v_bcols is None else [(S_V, v_bcols)])
                 + [(S_X, w.val(S_X))],
                 inc=[(S_X, 16)])
            v_ld = w.val(S_X)
            w.op("tensor", lambda e: e.transpose(pb[0].ap()[:, 0:128], iota_f[:, :],
                                                 iden_f[:, :]),
                 wait=[(S_X, v_ld), (S_V, v_iden)], inc=[(S_MM, 1)])
            v_tr = w.val(S_MM)
            w.op("vector", lambda e, r=rows, d=dst_c: e.tensor_copy(
                d[:, :], pb[0].ap()[:, 0:r]), wait=[(S_MM, v_tr)], inc=[(S_V, 1)],
                 spaced=True)
            v_bcols = w.val(S_V)

        def _bc(e):
            for i in range(4):
                e.matmul(pb[2].ap()[:, i * HD:(i + 1) * HD], onesf[:, :],
                         grow[:, i * HD:(i + 1) * HD], start=True, stop=True)
            for p in range(NP):
                e.matmul(pb[3].ap()[0:64, p:p + 1], onesq[0:1, 0:64],
                         temp_r[0:1, 2 * p:2 * p + 1], start=True, stop=True)
                mm = e.matmul(pb[3].ap()[64:128, p:p + 1], onesq[0:1, 0:64],
                              temp_r[0:1, 2 * p + 1:2 * p + 2], start=True, stop=True)
            return mm
        w.op("tensor", _bc, wait=[(S_V, v_bcols)], inc=[(S_MM, 1)])
        v_bc = w.val(S_MM)
        def _gbcev(e):
            e.tensor_copy(gbc[:, 0:64], pb[2].ap()[:, 0:64])
            e.tensor_copy(gbc[:, 64:128], pb[2].ap()[:, 0:64])
            e.tensor_copy(gbc[:, 128:192], pb[2].ap()[:, 64:128])
            e.tensor_copy(gbc[:, 192:256], pb[2].ap()[:, 64:128])
            e.tensor_copy(gbc2[:, 0:64], pb[2].ap()[:, 128:192])
            e.tensor_copy(gbc2[:, 64:128], pb[2].ap()[:, 128:192])
            e.tensor_copy(gbc2[:, 128:192], pb[2].ap()[:, 192:256])
            e.tensor_copy(gbc2[:, 192:256], pb[2].ap()[:, 192:256])
            return e.tensor_copy(tempq[:, :], pb[3].ap()[:, 0:NP])
        w.op("vector", _gbcev, wait=[(S_MM, v_bc)])
        w.op("vector", _init_scratch_late, inc=[(S_V, 1)], spaced=True)
        v_setup = w.val(S_V)

        conv_w1, conv_w2 = [], []
        for i in range(NE):
            w.op("gpsimd", lambda e, i=i: e.dma_start(w1b[i, :, :], w1t[i, :, :]),
                 wait=[(S_CONV, w.val(S_CONV))], inc=[(S_CONV, 16)])
            conv_w1.append(w.val(S_CONV))
        for i in range(NE):
            w.op("gpsimd", lambda e, i=i: e.dma_start(w2b[i, :, :], w2t[i, :, :]),
                 wait=[(S_CONV, w.val(S_CONV))], inc=[(S_CONV, 16)])
            conv_w2.append(w.val(S_CONV))
        for dst, src in ((qkvb_s, qkvwt), (p1b_s, p1wt), (p2b_s, p2wt)):
            w.op("gpsimd", lambda e, d=dst, s=src: e.dma_start(d[:, :], s[:, :]),
                 wait=[(S_CONV, w.val(S_CONV))], inc=[(S_CONV, 16)])
        conv_attn = w.val(S_CONV)

        def ln_tile(tile, xwait, stats_col=None):
            xt = x_tm[:, tile * D:(tile + 1) * D]
            w.op("vector", lambda e: e.reduce_sum(lncol[:, 0:1], xt, axis=AX),
                 wait=xwait, inc=[(S_V, 1)])
            w.op("scalar", lambda e: e.activation(sqtmp[:, :], xt, AF.Square),
                 wait=xwait, inc=[(S_A, 1)], spaced=True)
            v_sq = w.val(S_A)
            w.op("vector", lambda e: e.reduce_sum(lncol[:, 1:2], sqtmp[:, :], axis=AX),
                 wait=[(S_A, v_sq)], inc=[(S_V, 1)], spaced=True)
            v_red = w.val(S_V)
            w.op("vector", lambda e: e.tensor_scalar(lncol[:, 2:3], lncol[:, 0:1], 1.0 / D,
                                                     None, op0=ALU.mult), wait=[(S_V, v_red)])
            w.op("vector", lambda e: e.tensor_tensor(lncol[:, 3:4], lncol[:, 2:3],
                                                     lncol[:, 2:3], op=ALU.mult))
            w.op("vector", lambda e: e.scalar_tensor_tensor(
                lncol[:, 4:5], lncol[:, 1:2], 1.0 / D, lncol[:, 3:4],
                op0=ALU.mult, op1=ALU.subtract), inc=[(S_V, 1)], spaced=True)
            v_var = w.val(S_V)
            w.op("scalar", lambda e: e.activation(lncol[:, 5:6], lncol[:, 4:5], AF.Ln,
                                                  bias=consts[:, 0:1], scale=1.0),
                 wait=[(S_V, v_var)])
            w.op("scalar", lambda e: e.activation(fstats[:, 1:2], lncol[:, 5:6], AF.Exp,
                                                  scale=consts[:, 1:2]), inc=[(S_A, 1)],
                 spaced=True)
            v_rstd = w.val(S_A)
            w.op("vector", lambda e: e.tensor_copy(fstats[:, 0:1], lncol[:, 2:3]))
            if stats_col is not None:
                w.op("vector", lambda e, c=stats_col: e.tensor_copy(
                    astats[:, 2 * c:2 * c + 1], lncol[:, 2:3]))
                w.op("vector", lambda e, c=stats_col: e.tensor_copy(
                    astats[:, 2 * c + 1:2 * c + 2], fstats[:, 1:2]), wait=[(S_A, v_rstd)])
            w.op("vector", lambda e: e.tensor_scalar(
                xh_t[:, :], xt, fstats[:, 0:1], fstats[:, 1:2],
                op0=ALU.subtract, op1=ALU.mult), wait=[(S_A, v_rstd)], inc=[(S_V, 1)],
                 spaced=True)
            v_xh = w.val(S_V)
            def _tr(e):
                for c in range(KC):
                    mm = e.transpose(pt[0].ap()[:, c * 128:(c + 1) * 128],
                                     xh_t[:, c * 128:(c + 1) * 128], iden_b[:, :])
                return mm
            wt = [(S_V, v_xh)]
            if st["ptev"][0]:
                wt.append((S_V, st["ptev"][0]))
            w.op("tensor", _tr, wait=wt, inc=[(S_MM, 1)])
            v_trx = w.val(S_MM)
            dst = xhat_v[:, :, tile * 128:tile * 128 + 128]
            src = pt[0].ap().rearrange("p (c t) -> p c t", c=KC)
            w.op("vector", lambda e, d=dst, s=src: e.tensor_copy(d, s),
                 wait=[(S_MM, v_trx)], inc=[(S_V, 1)], spaced=True)
            st["ptev"][0] = w.val(S_V)
            return st["ptev"][0]

        def routing_tile(macro, tile, v_xhat):
            toff = (macro * MACT + tile) * 128
            wt = [(S_V, max(v_xhat, st["rt_done"])), (S_CONV, v_pw)]
            if st["tapv"]:
                wt.append((S_P, st["tapv"]))
            if st["diag_gate"]:
                wt.append((S_X, st["diag_gate"]))
            def _xa(e):
                for kc in range(KC):
                    mm = e.matmul(pb[5].ap()[:, 0:HD],
                                  xhat.ap()[:, kc * T + toff:kc * T + toff + 128],
                                  pwb[:, kc * HD:(kc + 1) * HD],
                                  start=(kc == 0), stop=(kc == KC - 1))
                return mm
            w.op("tensor", _xa, wait=wt, inc=[(S_MM, 1)])
            v1 = w.val(S_MM)
            w.op("vector", lambda e: e.tensor_copy(xa_sc[:, :], pb[5].ap()[:, 0:HD]),
                 wait=[(S_MM, v1)], inc=[(S_V, 1)], spaced=True)
            v2 = w.val(S_V)
            w.op("tensor", lambda e: e.transpose(pb[5].ap()[0:64, 128:256], xa_sc[:, :],
                                                 iden_f[:, :]), wait=[(S_V, v2)],
                 inc=[(S_MM, 1)])
            v3 = w.val(S_MM)
            w.op("vector", lambda e: e.tensor_copy(xat_sc[:, :], pb[5].ap()[0:64, 128:256]),
                 wait=[(S_MM, v3)], inc=[(S_V, 1)], spaced=True)
            v4 = w.val(S_V)
            w.op("tensor", lambda e: e.matmul(pb[5].ap()[:, 256:292], xat_sc[:, :],
                                              ucat_sb[:, :], start=True, stop=True),
                 wait=[(S_V, v4), (S_X, v_small)], inc=[(S_MM, 1)])
            v5 = w.val(S_MM)
            w.op("vector", lambda e: e.tensor_copy(rt_sc[:, 0:36], pb[5].ap()[:, 256:292]),
                 wait=[(S_MM, v5)])
            for r in range(3):
                zr = rt_sc[:, r * 12:r * 12 + 12]
                m8r = rt_sc[:, 36 + r * 8:36 + r * 8 + 8]
                zzr = rt_sc[:, 60 + r * 12:60 + r * 12 + 12]
                w.op("vector", lambda e, zr=zr, m8r=m8r: e.max(out=m8r, in_=zr))
                w.op("vector", lambda e, m8r=m8r: e.memset(m8r[:, 3:8], -1e30),
                     inc=[(S_V, 1)])
                vb = w.val(S_V)
                w.op("vector", lambda e, zr=zr, m8r=m8r, zzr=zzr: e.match_replace(
                    out=zzr, in_to_replace=m8r[:, 0:8], in_values=zr, imm_value=-1e30),
                    wait=[(S_V, vb)], inc=[(S_V, 1)], spaced=True)
                vmr = w.val(S_V)
                w.op("vector", lambda e, zr=zr, m8r=m8r, r=r: e.tensor_scalar(
                    rt_sc[:, 96 + r * 12:96 + r * 12 + 12], zr, m8r[:, 0:1], None,
                    op0=ALU.subtract), wait=[(S_V, vmr)])
                w.op("vector", lambda e, zr=zr, zzr=zzr, r=r: e.tensor_tensor(
                    rt_sc[:, 132 + r * 12:132 + r * 12 + 12], zr, zzr, op=ALU.not_equal))
            w.op("vector", lambda e: e.memset(vjunk[:, :], 0.0), inc=[(S_V, 1)])
            v6 = w.val(S_V)
            w.op("scalar", lambda e: e.activation(rt_sc[:, 96:132], rt_sc[:, 96:132],
                                                  AF.Exp, scale=INVTAU),
                 wait=[(S_V, v6)], inc=[(S_A, 1)], spaced=True)
            v7 = w.val(S_A)
            w.op("vector", lambda e: e.tensor_tensor(rt_sc[:, 96:132], rt_sc[:, 96:132],
                                                     rt_sc[:, 132:168], op=ALU.mult),
                 wait=[(S_A, v7)])
            mew3 = rt_sc.ap()[:, 96:132].rearrange("p (a b) -> p a b", a=3)
            w.op("vector", lambda e, m=mew3: e.reduce_sum(rt_sc[:, 168:171], m, axis=AX),
                 inc=[(S_V, 1)], spaced=True)
            v8 = w.val(S_V)
            w.op("vector", lambda e: e.reciprocal(rt_sc[:, 171:174], rt_sc[:, 168:171]),
                 wait=[(S_V, v8)], inc=[(S_V, 1)], spaced=True)
            v9 = w.val(S_V)
            col = tile * 36
            for r, s2 in ((0, 1.0), (1, 0.5), (2, 0.5)):
                w.op("vector", lambda e, r=r, s2=s2, col=col: e.tensor_scalar(
                    dw_sb[:, col + r * 12:col + r * 12 + 12],
                    rt_sc[:, 96 + r * 12:96 + r * 12 + 12],
                    rt_sc[:, 171 + r:172 + r], s2, op0=ALU.mult, op1=ALU.mult),
                    wait=[(S_V, v9)])
            w.op("vector", lambda e: e.memset(vjunk[:, :], 0.0), inc=[(S_V, 1)])
            v10 = w.val(S_V)
            w.op("tensor", lambda e, col=col: e.transpose(
                pb[5].ap()[0:12, 292:420], dw_sb[:, col + 24:col + 36], iden_f[:, :]),
                wait=[(S_V, v10)], inc=[(S_MM, 1)])
            v11 = w.val(S_MM)
            w.op("vector", lambda e, tile=tile: e.tensor_copy(
                dw3t[:, tile * 128:(tile + 1) * 128], pb[5].ap()[0:12, 292:420]),
                wait=[(S_MM, v11)], inc=[(S_V, 1)], spaced=True)
            st["rt_done"] = w.val(S_V)

        def ffn_macro(macro, xwait, sub="abc"):
            eMM, eV, eA = w.val(S_MM), w.val(S_V), w.val(S_A)
            first = st.get("ffn_call", 0) == 0 and macro == 0
            st["ffn_call"] = st.get("ffn_call", 0) + (1 if macro == NMACRO - 1 else 0)
            for t in range(MACT):
                v_xh = ln_tile(macro * MACT + t, xwait)
                routing_tile(macro, t, v_xh)
            v_dw = w.val(S_V)
            if first:
                tap("fdw", dw_sb.ap()[:, :], [(S_V, v_dw)])
                tap("fd3", dw3t.ap()[:, :], [(S_V, v_dw)])
                tap("fxh", xhat.ap()[:, :], [(S_V, v_dw)])
            if DIAG and st["seq"] == 0 and first:
                w.op("sync", lambda e: e.dma_start(diag_d["dw"][:, :], dw_sb.ap()[:, :]),
                     wait=[(S_V, v_dw), (S_X, w.val(S_X))], inc=[(S_X, 16)])
                w.op("sync", lambda e: e.dma_start(diag_d["xh"][:, :], xhat.ap()[:, :]),
                     wait=[(S_X, w.val(S_X))], inc=[(S_X, 16)])
                st["diag_gate"] = w.val(S_X)
            if "b" not in sub:
                w.op("vector", None, inc=[(S_V, 1)], spaced=True)
                return [(S_V, w.val(S_V))]

            for jw in range(NJW):
                for ei in range(NE):
                    s = st["w1_slice"]
                    par = s % 2
                    wt = [(S_CONV, conv_w1[ei]), (S_MM, eMM),
                          (S_V, max(eV, v_setup))]
                    if s - 2 in st["w1_end"]:
                        wt.append((S_MM, st["w1_end"][s - 2]))
                    src = w1b[ei, :, jw * JW:(jw + 1) * JW].rearrange(
                        "(kc p) j -> p kc j", p=128)
                    w.op("sync", lambda e, p=par, sr=src: e.dma_start(
                        wbuf1[p].ap().rearrange("p (kc j) -> p kc j", kc=KC), sr),
                        wait=wt, inc=[(S_W1[par], 16)])
                    v_w = w.val(S_W1[par])
                    st["w1_slice"] += 1
                    for t in range(MACT):
                        g = st["grp"]
                        bank = pb[g % 3]
                        toff = (macro * MACT + t) * 128
                        wt = [(S_W1[par], v_w), (S_V, v_dw)]
                        if st["tapv"]:
                            wt.append((S_P, st["tapv"]))
                        if g - 3 in st["mac"]:
                            wt.append((S_V, st["mac"][g - 3]))
                        if ei == 0 and st["gelu"][t]:
                            wt.append((S_MM, st["gelu"][t]))
                        def _g1(e, bank=bank, toff=toff, par=par):
                            for kc in range(KC):
                                mm = e.matmul(
                                    bank.ap()[:, 0:JW],
                                    xhat.ap()[:, kc * T + toff:kc * T + toff + 128],
                                    wbuf1[par].ap()[:, kc * JW:(kc + 1) * JW],
                                    start=(kc == 0), stop=(kc == KC - 1))
                            return mm
                        w.op("tensor", _g1, wait=wt, inc=[(S_MM, 1)])
                        v_g = w.val(S_MM)
                        dwc = dw_sb[:, t * 36 + ei:t * 36 + ei + 1]
                        hs = h_jw[:, t * JW:(t + 1) * JW]
                        if ei == 0:
                            w.op("vector", lambda e, b=bank, d=dwc, h=hs: e.tensor_scalar(
                                h, b.ap()[:, 0:JW], d, None, op0=ALU.mult),
                                wait=[(S_MM, v_g)], inc=[(S_V, 1)])
                        else:
                            w.op("vector", lambda e, b=bank, d=dwc, h=hs:
                                 e.scalar_tensor_tensor(h, b.ap()[:, 0:JW], d, h,
                                                        op0=ALU.mult, op1=ALU.add),
                                 wait=[(S_MM, v_g)], inc=[(S_V, 1)],
                                 spaced=(ei == NE - 1))
                        st["mac"][g] = w.val(S_V)
                        st["grp"] += 1
                    st["w1_end"][s] = w.val(S_MM)
                if DIAG and st["seq"] == 0 and first and jw == 0:
                    vmacs = w.val(S_V)
                    w.op("sync", lambda e: e.dma_start(diag_d["hj"][:, :], h_jw.ap()[:, :]),
                         wait=[(S_V, vmacs), (S_X, w.val(S_X))], inc=[(S_X, 16)])
                    st["diag_gate"] = w.val(S_X)
                for t in range(MACT):
                    v_mac = st["mac"][st["grp"] - MACT + t]
                    w.op("scalar", lambda e, t=t: e.activation(
                        h_jw[:, t * JW:(t + 1) * JW], h_jw[:, t * JW:(t + 1) * JW],
                        AF.Gelu),
                        wait=[(S_V, v_mac)] + ([(S_X, st["diag_gate"])]
                                               if st["diag_gate"] else []),
                        inc=[(S_A, 1)], spaced=True)
                    v_gl = w.val(S_A)
                    wt = [(S_A, v_gl)]
                    if st.get("gtev"):
                        wt.append((S_V, st["gtev"]))
                    def _trg(e, t=t):
                        for c in range(JW // 128):
                            mm = e.transpose(
                                pb[3].ap()[:, c * 128:(c + 1) * 128],
                                h_jw[:, t * JW + c * 128:t * JW + (c + 1) * 128],
                                iden_f[:, :])
                        return mm
                    w.op("tensor", _trg, wait=wt, inc=[(S_MM, 1)])
                    v_tg = w.val(S_MM)
                    st["gelu"][t] = v_tg
                    gdst = g_v[:, (JW // 128) * jw:(JW // 128) * (jw + 1),
                               t * 128:t * 128 + 128]
                    gsrc = pb[3].ap().rearrange("p (c t) -> p c t", c=4)
                    w.op("vector", lambda e, d=gdst, s=gsrc: e.tensor_copy(d, s),
                         wait=[(S_MM, v_tg)], inc=[(S_V, 1)], spaced=True)
                    st["gtev"] = w.val(S_V)
            v_gdone = w.val(S_V)
            if first:
                tap("fg", g_fm.ap()[:, :], [(S_V, v_gdone)])
                tap("fh", h_jw.ap()[:, :], [(S_V, v_gdone)])
            if "c" not in sub:
                w.op("vector", None, inc=[(S_V, 1)], spaced=True)
                return [(S_V, w.val(S_V))]

            v_prev = 0
            for t in range(MACT):
                wt = [(S_CONV, v_pw), (S_V, max(v_gdone, v_prev)), (S_X, v_small)]
                def _b2(e, t=t):
                    e.matmul(pb[3].ap()[:, :], dw3t[:, t * 128:(t + 1) * 128],
                             b2_sb[:, 0:512], start=True, stop=True)
                    return e.matmul(pb[4].ap()[:, :], dw3t[:, t * 128:(t + 1) * 128],
                                    b2_sb[:, 512:1024], start=True, stop=True)
                w.op("tensor", _b2, wait=wt, inc=[(S_MM, 1)])
                v_b2 = w.val(S_MM)
                xt = x_tm[:, (macro * MACT + t) * D:(macro * MACT + t + 1) * D]
                w.op("vector", lambda e, xt=xt: e.tensor_tensor(
                    xt[:, 0:512], xt[:, 0:512], pb[3].ap()[:, :], op=ALU.add),
                    wait=[(S_MM, v_b2)] + ([(S_X, st["diag_gate"])] if st["diag_gate"] else []))
                w.op("vector", lambda e, xt=xt: e.tensor_tensor(
                    xt[:, 512:1024], xt[:, 512:1024], pb[4].ap()[:, :], op=ALU.add),
                    wait=[(S_MM, v_b2)], inc=[(S_V, 1)])
                v_prev = w.val(S_V)
            v_b2all = w.val(S_V)

            for ow in range(2):
                for ei in range(NE):
                    vh = [None, None]
                    for half in range(2):
                        s = st["w2_slice"]
                        par = s % 2
                        wt = [(S_CONV, conv_w2[ei]), (S_MM, eMM),
                              (S_V, max(v_gdone, v_setup))]
                        if s - 2 in st["w2_end"]:
                            wt.append((S_MM, st["w2_end"][s - 2]))
                        src = w2b[ei, half * 2048:(half + 1) * 2048,
                                  ow * 512:(ow + 1) * 512].rearrange(
                                      "(mc p) o -> p mc o", p=128)
                        w.op("sync", lambda e, p=par, sr=src: e.dma_start(
                            wbuf2[p].ap().rearrange("p (mc o) -> p mc o", mc=16), sr),
                            wait=wt, inc=[(S_W2[par], 16)])
                        vh[half] = (par, w.val(S_W2[par]), s)
                        st["w2_slice"] += 1
                    grp_base = st["grp"]
                    for tb4 in range(2):
                        for half in range(2):
                            par, v_w, s = vh[half]
                            for tl in range(4):
                                t = tb4 * 4 + tl
                                g = grp_base + t
                                bank = pb[g % 5]
                                toff = t * 128
                                wt = [(S_W2[par], v_w)]
                                if st["tapv"]:
                                    wt.append((S_P, st["tapv"]))
                                if half == 0:
                                    wt.append((S_V, max(v_gdone, v_b2all)))
                                    if g - 5 in st["mac"]:
                                        wt.append((S_V, st["mac"][g - 5]))
                                def _g2(e, bank=bank, toff=toff, par=par, half=half):
                                    for mc in range(16):
                                        mm = e.matmul(
                                            bank.ap()[:, :],
                                            g_fm.ap()[:, (half * 16 + mc) * 1024 + toff:
                                                      (half * 16 + mc) * 1024 + toff + 128],
                                            wbuf2[par].ap()[:, mc * 512:(mc + 1) * 512],
                                            start=(half == 0 and mc == 0),
                                            stop=(half == 1 and mc == 15))
                                    return mm
                                inc = [(S_MM, 1)] if half == 1 else []
                                w.op("tensor", _g2, wait=wt, inc=inc)
                                if half == 1:
                                    v_g = w.val(S_MM)
                                    dwc = dw_sb[:, t * 36 + 12 + ei:t * 36 + 12 + ei + 1]
                                    xt = x_tm[:, (macro * MACT + t) * D + ow * 512:
                                              (macro * MACT + t) * D + ow * 512 + 512]
                                    w.op("vector", lambda e, b=bank, d=dwc, xt=xt:
                                         e.scalar_tensor_tensor(xt, b.ap()[:, :], d, xt,
                                                                op0=ALU.mult, op1=ALU.add),
                                         wait=[(S_MM, v_g)], inc=[(S_V, 1)])
                                    st["mac"][g] = w.val(S_V)
                    st["grp"] += MACT
                    st["w2_end"][vh[0][2]] = w.val(S_MM)
                    st["w2_end"][vh[1][2]] = w.val(S_MM)
            w.op("vector", None, inc=[(S_V, 1)], spaced=True)
            return [(S_V, w.val(S_V))]

        def attn_seq(xwait):
            eMM, eV, eA = w.val(S_MM), w.val(S_V), w.val(S_A)
            tap("xin", x_tm.ap()[:, :], list(xwait))
            for t in range(NT):
                ln_tile(t, xwait, stats_col=t)
            v_xh = w.val(S_V)
            tap("xhat", xhat.ap()[:, :], [(S_V, v_xh)])

            def wq_dma(cc, extra=()):
                s = st["wq_slice"]
                par = s % 2
                wt = [(S_CONV, conv_attn), (S_MM, eMM), (S_V, v_setup)] + list(extra)
                if s - 2 in st["wq_end"]:
                    wt.append((S_MM, st["wq_end"][s - 2]))
                src = qkvb_s[:, cc * 128:(cc + 1) * 128].rearrange("(kc p) c -> p kc c", p=128)
                w.op("sync", lambda e, p=par, sr=src: e.dma_start(
                    wbufa[p].ap().rearrange("p (kc c) -> p kc c", c=256)[:, :, 0:128], sr),
                    wait=wt, inc=[(S_WQ[par], 16)])
                st["wq_slice"] += 1
                return par, w.val(S_WQ[par]), s

            def qkv_cc(cc, par, v_w, evac):
                for tb in range(4):
                    g = st["grp"]
                    bank = pb[g % 3]
                    wt = [(S_WQ[par], v_w), (S_V, v_xh)]
                    if g - 3 in st["mac"]:
                        wt.append((S_V, st["mac"][g - 3]))
                    if g - 3 in st.get("aev", {}):
                        wt.append((S_A, st["aev"][g - 3]))
                    def _gq(e, bank=bank, tb=tb, par=par):
                        for kc in range(KC):
                            mm = e.matmul(bank.ap()[:, :],
                                          qw_lhs(par, kc),
                                          xhat.ap()[:, kc * T + tb * 512:kc * T + tb * 512 + 512],
                                          start=(kc == 0), stop=(kc == KC - 1))
                        return mm
                    w.op("tensor", _gq, wait=wt, inc=[(S_MM, 1)])
                    evac(tb, bank, w.val(S_MM), g)
                    st["grp"] += 1

            def qw_lhs(par, kc):
                return wbufa[par].ap()[:, kc * 256:kc * 256 + 128]

            st.setdefault("aev", {})
            if ATTN_SUB < 1:
                return xwait
            for cc in range(8):
                par, v_w, s = wq_dma(cc)
                def evq(tb, bank, v_g, g, cc=cc):
                    w.op("scalar", lambda e, b=bank, tb=tb, cc=cc: e.activation(
                        q_fm.ap()[:, cc * T + tb * 512:cc * T + tb * 512 + 512],
                        b.ap()[:, :], AF.Identity, bias=qkvb_c[:, cc:cc + 1], scale=1.0),
                        wait=[(S_MM, v_g)], inc=[(S_A, 1)], spaced=True)
                    st["aev"][g] = w.val(S_A)
                qkv_cc(cc, par, v_w, evq)
                st["wq_end"][s] = w.val(S_MM)
            v_qdone = w.val(S_A)
            tap("q", q_fm.ap()[:, :], [(S_A, v_qdone)])

            def ln_kv(buf_tm, gdup, sums_off):
                w.op("vector", lambda e, b=buf_tm: e.tensor_tensor(
                    sqk[:, :], b.ap()[:, :], b.ap()[:, :], op=ALU.mult),
                    wait=([(S_P, st["tapv"])] if st["tapv"] else []))
                for qt in range(NT):
                    w.op("vector", lambda e, b=buf_tm, qt=qt: e.reduce_sum(
                        kvstat[:, qt * 2:qt * 2 + 2],
                        b.ap()[:, qt * 128:(qt + 1) * 128].rearrange(
                            "p (h d) -> p h d", h=2), axis=AX))
                    last = qt == NT - 1
                    w.op("vector", lambda e, qt=qt: e.reduce_sum(
                        kvstat[:, 32 + qt * 2:32 + qt * 2 + 2],
                        sqk.ap()[:, qt * 128:(qt + 1) * 128].rearrange(
                            "p (h d) -> p h d", h=2), axis=AX),
                        inc=[(S_V, 1)] if last else (), spaced=last)
                vred = w.val(S_V)
                w.op("vector", lambda e: e.tensor_scalar(
                    kvstat[:, 64:96], kvstat[:, 0:32], 1.0 / HD, None, op0=ALU.mult),
                    wait=[(S_V, vred)])
                w.op("vector", lambda e: e.tensor_tensor(
                    kvstat[:, 0:32], kvstat[:, 64:96], kvstat[:, 64:96], op=ALU.mult))
                w.op("vector", lambda e: e.scalar_tensor_tensor(
                    kvstat[:, 32:64], kvstat[:, 32:64], 1.0 / HD, kvstat[:, 0:32],
                    op0=ALU.mult, op1=ALU.subtract), inc=[(S_V, 1)], spaced=True)
                vvar = w.val(S_V)
                w.op("scalar", lambda e: e.activation(kvstat[:, 32:64], kvstat[:, 32:64],
                                                      AF.Ln, bias=consts[:, 0:1], scale=1.0),
                     wait=[(S_V, vvar)])
                w.op("scalar", lambda e: e.activation(kvstat[:, 32:64], kvstat[:, 32:64],
                                                      AF.Exp, scale=consts[:, 1:2]),
                     inc=[(S_A, 1)], spaced=True)
                vrs = w.val(S_A)
                for qt in range(NT):
                    for hf in range(2):
                        mcol = kvstat[:, 64 + qt * 2 + hf:64 + qt * 2 + hf + 1]
                        rcol = kvstat[:, 32 + qt * 2 + hf:32 + qt * 2 + hf + 1]
                        sl = slice(qt * 128 + hf * 64, qt * 128 + hf * 64 + 64)
                        w.op("vector", lambda e, b=buf_tm, sl=sl, m=mcol, r=rcol:
                             e.tensor_scalar(sqk[:, sl], b.ap()[:, sl], m, r,
                                             op0=ALU.subtract, op1=ALU.mult),
                             wait=[(S_A, vrs)])
                    w.op("vector", lambda e, qt=qt, g=gdup: e.tensor_tensor(
                        sqk[:, qt * 128:(qt + 1) * 128],
                        sqk[:, qt * 128:(qt + 1) * 128], g[:, 0:128], op=ALU.mult))
                    w.op("vector", lambda e, b=buf_tm, qt=qt, g=gdup: e.tensor_tensor(
                        b.ap()[:, qt * 128:(qt + 1) * 128],
                        sqk[:, qt * 128:(qt + 1) * 128], g[:, 128:256], op=ALU.add))
                w.op("vector", lambda e: e.memset(vjunk[:, :], 0.0), inc=[(S_V, 1)])
                return w.val(S_V)

            def pair_side(cc, dst_pair, dst_tm, bias_cols, gdup):
                par, v_w, s = wq_dma(cc)
                def evk(tb, bank, v_g, g, cc=cc, dp=dst_pair):
                    wt = [(S_MM, v_g)]
                    if st.get("pair_free", 0):
                        wt.append((S_MM, st["pair_free"]))
                    if st["tapv"]:
                        wt.append((S_P, st["tapv"]))
                    w.op("scalar", lambda e, b=bank, tb=tb, cc=cc, dp=dp: e.activation(
                        dp.ap()[:, tb * 512:tb * 512 + 512], b.ap()[:, :], AF.Identity,
                        bias=qkvb_c[:, cc:cc + 1], scale=1.0), wait=wt, inc=[(S_A, 1)],
                        spaced=True)
                    st["aev"][g] = w.val(S_A)
                qkv_cc(cc, par, v_w, evk)
                st["wq_end"][s] = w.val(S_MM)
                v_ev = w.val(S_A)
                if cc == 8:
                    tap("kraw0", dst_pair.ap()[:, :], [(S_A, v_ev)])
                for b8 in range(2):
                    pti = pt[b8]
                    wt = [(S_A, v_ev)]
                    if st["ptev"][b8]:
                        wt.append((S_V, st["ptev"][b8]))
                    if st["tapv"]:
                        wt.append((S_P, st["tapv"]))
                    def _trk(e, b8=b8, pti=pti, dp=dst_pair):
                        for c in range(8):
                            qt = b8 * 8 + c
                            mm = e.transpose(pti.ap()[:, c * 128:(c + 1) * 128],
                                             dp.ap()[:, qt * 128:(qt + 1) * 128],
                                             iden_b[:, :])
                        return mm
                    w.op("tensor", _trk, wait=wt, inc=[(S_MM, 1)])
                    v_t = w.val(S_MM)
                    w.op("vector", lambda e, b8=b8, pti=pti, dt=dst_tm: e.tensor_copy(
                        dt.ap()[:, b8 * 1024:(b8 + 1) * 1024], pti.ap()[:, :]),
                        wait=[(S_MM, v_t)], inc=[(S_V, 1)])
                    st["ptev"][b8] = w.val(S_V)
                return ln_kv(dst_tm, gdup, 0)

            if ATTN_SUB < 2:
                return xwait
            for p in range(NP):
                v_k = pair_side(8 + p, kpair, k_tm, None, gbc)
                if p == 0:
                    tap("ktm0", k_tm.ap()[:, :], [(S_V, v_k)])
                    tap("kvstat0", kvstat.ap()[:, :], [(S_V, v_k)])
                v_v = pair_side(16 + p, vpair, v_tm, None, gbc2)
                if p == 0:
                    tap("vtm0", v_tm.ap()[:, :], [(S_V, v_v)])
                def _kv(e, p=p):
                    for hf in range(2):
                        for qt in range(NT):
                            mm = e.matmul(
                                pb[5].ap()[hf * 64:hf * 64 + 64, p * 64:(p + 1) * 64],
                                k_tm.ap()[:, qt * 128 + hf * 64:qt * 128 + hf * 64 + 64],
                                v_tm.ap()[:, qt * 128 + hf * 64:qt * 128 + hf * 64 + 64],
                                start=(qt == 0), stop=(qt == NT - 1))
                    return mm
                w.op("tensor", _kv, wait=[(S_V, max(v_k, v_v))], inc=[(S_MM, 1)])
                st["pair_free"] = w.val(S_MM)
            v_kvmm = w.val(S_MM)
            for p in range(NP):
                w.op("vector", lambda e, p=p: e.scalar_tensor_tensor(
                    kvsb[:, p * 64:(p + 1) * 64], pb[5].ap()[:, p * 64:(p + 1) * 64],
                    tempq[:, p:p + 1], attnb_sb[:, p * 64:(p + 1) * 64],
                    op0=ALU.mult, op1=ALU.add), wait=[(S_MM, v_kvmm)])
            w.op("vector", lambda e: e.memset(vjunk[:, :], 0.0), inc=[(S_V, 1)])
            v_kv = w.val(S_V)
            tap("kv", kvsb.ap()[:, :], [(S_V, v_kv)])
            if ATTN_SUB < 3:
                return xwait

            for p in range(NP):
                for hf in range(2):
                    for tb in range(4):
                        g = st["grp"]
                        bank = pb[g % 3]
                        wt = [(S_V, v_kv), (S_A, v_qdone)]
                        if g - 3 in st["mac"]:
                            wt.append((S_V, st["mac"][g - 3]))
                        if g - 3 in st["aev"]:
                            wt.append((S_A, st["aev"][g - 3]))
                        w.op("tensor", lambda e, b=bank, p=p, hf=hf, tb=tb: e.matmul(
                            b.ap()[0:64, :],
                            kvsb[hf * 64:hf * 64 + 64, p * 64:(p + 1) * 64],
                            q_fm.ap()[hf * 64:hf * 64 + 64,
                                      p * T + tb * 512:p * T + tb * 512 + 512],
                            start=True, stop=True), wait=wt, inc=[(S_MM, 1)])
                        v_g = w.val(S_MM)
                        w.op("vector", lambda e, b=bank, p=p, hf=hf, tb=tb: e.tensor_tensor(
                            u_fm.ap()[hf * 64:hf * 64 + 64,
                                      p * T + tb * 512:p * T + tb * 512 + 512],
                            b.ap()[0:64, :],
                            xhat.ap()[hf * 64:hf * 64 + 64,
                                      p * T + tb * 512:p * T + tb * 512 + 512],
                            op=ALU.add), wait=[(S_MM, v_g)], inc=[(S_V, 1)],
                            spaced=True)
                        st["mac"][g] = w.val(S_V)
                        st["grp"] += 1
            v_u = w.val(S_V)
            tap("u", u_fm.ap()[:, :], [(S_V, v_u)])
            if ATTN_SUB < 4:
                return xwait

            for sl in range(8):
                par = st["wq_slice"] % 2
                s = st["wq_slice"]
                wt = [(S_CONV, conv_attn), (S_V, v_setup)]
                if s - 2 in st["wq_end"]:
                    wt.append((S_MM, st["wq_end"][s - 2]))
                src = p1b_s[:, sl * 128:(sl + 1) * 128].rearrange("(kc p) c -> p kc c", p=128)
                w.op("sync", lambda e, p=par, sr=src: e.dma_start(
                    wbufa[p].ap().rearrange("p (kc c) -> p kc c", c=256)[:, :, 0:128], sr),
                    wait=wt, inc=[(S_WQ[par], 16)])
                v_w = w.val(S_WQ[par])
                st["wq_slice"] += 1
                for tb in range(4):
                    g = st["grp"]
                    bank = pb[g % 3]
                    wt = [(S_WQ[par], v_w), (S_V, v_u)]
                    if g - 3 in st["mac"]:
                        wt.append((S_V, st["mac"][g - 3]))
                    if g - 3 in st["aev"]:
                        wt.append((S_A, st["aev"][g - 3]))
                    def _gp1(e, bank=bank, tb=tb, par=par):
                        for kc in range(KC):
                            mm = e.matmul(bank.ap()[:, :], qw_lhs(par, kc),
                                          u_fm.ap()[:, kc * T + tb * 512:kc * T + tb * 512 + 512],
                                          start=(kc == 0), stop=(kc == KC - 1))
                        return mm
                    w.op("tensor", _gp1, wait=wt, inc=[(S_MM, 1)])
                    v_g = w.val(S_MM)
                    w.op("scalar", lambda e, b=bank, sl=sl, tb=tb: e.activation(
                        hp_fm.ap()[:, sl * T + tb * 512:sl * T + tb * 512 + 512],
                        b.ap()[:, :], AF.Gelu, bias=p1b_c[:, sl:sl + 1], scale=1.0),
                        wait=[(S_MM, v_g)], inc=[(S_A, 1)], spaced=True)
                    st["aev"][g] = w.val(S_A)
                    st["grp"] += 1
                st["wq_end"][s] = w.val(S_MM)
            v_hp = w.val(S_A)
            tap("hp", hp_fm.ap()[:, :], [(S_A, v_hp)])
            if ATTN_SUB < 5:
                return xwait

            for sl in range(4):
                par = st["wq_slice"] % 2
                s = st["wq_slice"]
                wt = [(S_CONV, conv_attn), (S_V, v_setup)]
                if s - 2 in st["wq_end"]:
                    wt.append((S_MM, st["wq_end"][s - 2]))
                src = p2b_s[:, sl * 256:(sl + 1) * 256].rearrange("(kc p) c -> p kc c", p=128)
                w.op("sync", lambda e, p=par, sr=src: e.dma_start(
                    wbufa[p].ap().rearrange("p (kc c) -> p kc c", c=256), sr),
                    wait=wt, inc=[(S_WQ[par], 16)])
                v_w = w.val(S_WQ[par])
                st["wq_slice"] += 1
                for t in range(NT):
                    g = st["grp"]
                    bank = pb[g % 3]
                    wt = [(S_WQ[par], v_w), (S_A, v_hp), (S_CONV, v_pw)]
                    if g - 3 in st["mac"]:
                        wt.append((S_V, st["mac"][g - 3]))
                    if g - 3 in st["aev"]:
                        wt.append((S_A, st["aev"][g - 3]))
                    def _gp2(e, bank=bank, t=t, par=par, sl=sl):
                        for kc in range(KC):
                            e.matmul(bank.ap()[:, 0:256],
                                     hp_fm.ap()[:, kc * T + t * 128:kc * T + t * 128 + 128],
                                     wbufa[par].ap()[:, kc * 256:(kc + 1) * 256],
                                     start=(kc == 0), stop=False)
                        return e.matmul(bank.ap()[:, 0:256], onesrow[:, :],
                                        p2brow_b[:, sl * 256:(sl + 1) * 256],
                                        start=False, stop=True)
                    w.op("tensor", _gp2, wait=wt, inc=[(S_MM, 1)])
                    v_g = w.val(S_MM)
                    xt = x_tm[:, t * D + sl * 256:t * D + sl * 256 + 256]
                    w.op("vector", lambda e, t=t, xt=xt: e.tensor_scalar(
                        xhrec[:, 0:256], xt,
                        astats[:, 2 * t:2 * t + 1], astats[:, 2 * t + 1:2 * t + 2],
                        op0=ALU.subtract, op1=ALU.mult))
                    w.op("vector", lambda e, b=bank, xt=xt: e.tensor_tensor(
                        xt, xt, b.ap()[:, 0:256], op=ALU.add),
                        wait=[(S_MM, v_g)] + ([(S_X, st["diag_gate"])] if st["diag_gate"] else []))
                    w.op("vector", lambda e, xt=xt: e.tensor_tensor(
                        xt, xt, xhrec[:, 0:256], op=ALU.add),
                        inc=[(S_V, 1)])
                    st["mac"][g] = w.val(S_V)
                    st["grp"] += 1
                st["wq_end"][s] = w.val(S_MM)
            w.op("vector", None, inc=[(S_V, 1)], spaced=True)
            return [(S_V, w.val(S_V))]


        for seq in range(NSEQ):
            st["seq"] = seq
            wt = [(S_X, st["store_done"])] if seq else []
            xsrc = x_in[seq * T:(seq + 1) * T, :].rearrange("(t p) f -> p t f", p=128)
            w.op("sync", lambda e, s=xsrc: e.dma_start(
                x_tm.ap().rearrange("p (t f) -> p t f", t=NT), s),
                wait=wt + [(S_X, w.val(S_X))], inc=[(S_X, 16)])
            xwait = [(S_X, w.val(S_X)), (S_V, v_setup)]

            if "ffn1" in STAGES:
                for m in range(NMACRO):
                    xwait = ffn_macro(m, xwait)
                    diag(xwait)
            if TAP and st["seq"] == 0 and "ffn1" in STAGES:
                tap("fx0", x_tm.ap()[:, :], list(xwait))
            if "attn" in STAGES:
                xwait = attn_seq(xwait)
                diag(xwait)
            if "ffn2" in STAGES:
                for m in range(NMACRO):
                    xwait = ffn_macro(m, xwait, sub=FFN2_SUB)
                    diag(xwait)

            v_last = w.val(S_V)
            xdst = out_d[seq * T:(seq + 1) * T, :].rearrange("(t p) f -> p t f", p=128)
            w.op("sync", lambda e, d=xdst: e.dma_start(
                d, x_tm.ap().rearrange("p (t f) -> p t f", t=NT)),
                wait=[(S_V, v_last), (S_X, w.val(S_X))], inc=[(S_X, 16)])
            st["store_done"] = w.val(S_X)

        w.op("sync", None, wait=[(S_X, w.val(S_X)), (S_P, w.val(S_P))])

    block = es.enter_context(nc.Block())
    walks = []

    def mk(name):
        def body(eng):
            ww = Walk(name, eng)
            walk(ww)
            walks.append(ww)
        return body

    block.sync(mk("sync"))
    block.tensor(mk("tensor"))
    block.vector(mk("vector"))
    block.scalar(mk("scalar"))
    block.gpsimd(mk("gpsimd"))
    es.close()
    ref = dict(walks[0].c)
    for ww in walks[1:]:
        assert dict(ww.c) == ref, "walk divergence"
    return nc


_NC_CACHE = {}


def kernel(**inputs):
    d = host_prep(inputs)
    x = d.pop("__x")
    key = (STAGES, TAP, ATTN_SUB, FFN2_SUB, DIAG)
    if key not in _NC_CACHE:
        _NC_CACHE[key] = build_program()
    nc = _NC_CACHE[key]
    in_maps = []
    for c in range(NCORE):
        m = dict(d)
        m["x"] = x[c * NSEQ * T:(c + 1) * NSEQ * T]
        in_maps.append(m)
    res = run_bass_kernel_spmd(nc, in_maps, core_ids=list(range(NCORE)))
    out = np.concatenate([res.results[c]["out"] for c in range(NCORE)], axis=0)
    return np.ascontiguousarray(out.reshape(32, T, D))
